# version 02 (final)
# baseline (speedup 1.0000x reference)
import contextlib
import numpy as np
import ml_dtypes
import concourse.bass as bass
import concourse.mybir as mybir
from concourse.ap import AP
from concourse.bass_utils import run_bass_kernel_spmd

F32 = mybir.dt.float32
BF16 = mybir.dt.bfloat16
AF = mybir.ActivationFunctionType
ALU = mybir.AluOpType
AXX = mybir.AxisListType.X

B_, N_, D_ = 2, 8192, 1024
TOK = 2048
NB = 16
OFFS = [(1 << j) * tau for j in range(11) for tau in range(4)]
LTS_FULL = [(3 * t, 3) for t in range(42)] + [(126, 2)]
FAR_LTN = None
FAR_PARTS = 7
LTS = LTS_FULL if FAR_LTN is None else LTS_FULL[:FAR_LTN]
LN_EPS_1 = 1e-5
LN_EPS_FFN = 1e-3

DBG = False
STAGE = 9
LN1_ON = True
DBG_GN4 = True


def _hostc():
    c = {}
    c["ident_b"] = np.eye(128, dtype=ml_dtypes.bfloat16)
    c["ones_b"] = np.ones((1, 128), dtype=ml_dtypes.bfloat16)
    c["ones_f"] = np.ones((1, 128), dtype=np.float32)
    ind_n = np.zeros((44, 512), dtype=np.float32)
    ind_f = np.zeros((44, 64), dtype=np.float32)
    for i, o in enumerate(OFFS):
        if o <= 256:
            ind_n[i, 128 + o] = 1.0
        else:
            ind_f[i, 16 + o // 128] = 1.0
    c["ind_n"] = ind_n.astype(ml_dtypes.bfloat16)
    c["ind_f"] = ind_f.astype(ml_dtypes.bfloat16)
    rowsel = np.zeros((16, 16 * 128), dtype=np.float32)
    for h in range(16):
        rowsel[h, h * 128:(h + 1) * 128] = 1.0
    c["rowsel_in"] = rowsel.astype(ml_dtypes.bfloat16)
    sel2 = np.zeros((16, 8 * 128), dtype=np.float32)
    for hp2 in range(8):
        sel2[2 * hp2, hp2 * 128:hp2 * 128 + 64] = 1.0
        sel2[2 * hp2 + 1, hp2 * 128 + 64:hp2 * 128 + 128] = 1.0
    c["sel2_in"] = sel2.astype(ml_dtypes.bfloat16)
    return c


def build():
    nc = bass.Bass("TRN2", target_bir_lowering=False, debug=False)

    x_in = nc.dram_tensor("x_in", [TOK, D_], F32, kind="ExternalInput").ap()
    w_qkv = nc.dram_tensor("w_qkv", [D_, 3 * D_], F32, kind="ExternalInput").ap()
    b_qkv = nc.dram_tensor("b_qkv", [128, 24], F32, kind="ExternalInput").ap()
    bv_row = nc.dram_tensor("bv_row", [1, D_], F32, kind="ExternalInput").ap()
    w_out_d = nc.dram_tensor("w_out_d", [D_, D_], F32, kind="ExternalInput").ap()
    b_out = nc.dram_tensor("b_out", [1, D_], F32, kind="ExternalInput").ap()
    w_gate = nc.dram_tensor("w_gate", [D_, D_], F32, kind="ExternalInput").ap()
    b_gate = nc.dram_tensor("b_gate", [128, 8], F32, kind="ExternalInput").ap()
    pos_b = nc.dram_tensor("pos_b", [44, 16], F32, kind="ExternalInput").ap()
    w_ffn = nc.dram_tensor("w_ffn", [D_, D_], F32, kind="ExternalInput").ap()
    g_ffn = nc.dram_tensor("g_ffn", [128, 8], F32, kind="ExternalInput").ap()
    be_ffn = nc.dram_tensor("be_ffn", [128, 8], F32, kind="ExternalInput").ap()
    ident_b = nc.dram_tensor("ident_b", [128, 128], BF16, kind="ExternalInput").ap()
    ones_b = nc.dram_tensor("ones_b", [1, 128], BF16, kind="ExternalInput").ap()
    ones_f = nc.dram_tensor("ones_f", [1, 128], F32, kind="ExternalInput").ap()
    ind_n = nc.dram_tensor("ind_n", [44, 512], BF16, kind="ExternalInput").ap()
    ind_f = nc.dram_tensor("ind_f", [44, 64], BF16, kind="ExternalInput").ap()
    sel2_in = nc.dram_tensor("sel2_in", [16, 8 * 128], BF16, kind="ExternalInput").ap()
    rowsel_in = nc.dram_tensor("rowsel_in", [16, 16 * 128], BF16, kind="ExternalInput").ap()

    y_out = nc.dram_tensor("y_out", [TOK, D_], F32, kind="ExternalOutput").ap()

    repn_d = nc.dram_tensor("repn_d", [128, 16 * 512], BF16)
    repf_d = nc.dram_tensor("repf_d", [128, 16 * 64], BF16)
    kown_d = nc.dram_tensor("kown_d", [128, 8 * 2048], BF16)
    vown_d = nc.dram_tensor("vown_d", [128, 16 * 1040], BF16)
    kall_d = nc.dram_tensor("kall_d", [8, 128, 8 * 2048], BF16)
    vall_d = nc.dram_tensor("vall_d", [8, 128, 16 * 1040], BF16)
    gate_d = nc.dram_tensor("gate_d", [128, 8 * 2048], BF16)
    gfar_d = nc.dram_tensor("gfar_d", [16, 65, 2048], BF16)
    gnear_d = nc.dram_tensor("gnear_d", [16, 65, 2048], BF16)
    gmxs_d = nc.dram_tensor("gmxs_d", [128, 8], F32)
    gmxall_d = nc.dram_tensor("gmxall_d", [8, 128, 8], F32)
    gmaxo_d = nc.dram_tensor("gmaxo_d", [1, 1], F32)

    dbg = {}
    if DBG:
        def dbgt(name, shape):
            dbg[name] = nc.dram_tensor(name, shape, F32, kind="ExternalOutput").ap()
        dbgt("d_q", [128, 8 * 2048])
        dbgt("d_k", [128, 8 * 5120])
        dbgt("d_v", [128, 4 * 1040])
        dbgt("d_gn", [65, 2048])
        dbgt("d_x1", [TOK, D_])
        dbgt("d_pf", [128, 8 * 2048])
        dbgt("d_za", [16, 2048])
        dbgt("d_g15", [65, 2048])
        dbgt("d_gf15", [65, 2048])
        dbgt("d_st", [128, 32])
        dbgt("d_xn", [128, 8 * 2048])
        dbgt("d_wq", [128, 16 * 512])
        dbgt("d_fs", [1, 8])
        dbgt("d_mx", [128, 16])

    with contextlib.ExitStack() as ctx:
        E = ctx.enter_context

        b_qkv_sb = E(nc.sbuf_tensor("b_qkv_sb", [128, 24], F32))
        b_qkv_sc = E(nc.sbuf_tensor("b_qkv_sc", [128, 24], F32))
        b_gate_sb = E(nc.sbuf_tensor("b_gate_sb", [128, 8], F32))
        id_b = E(nc.sbuf_tensor("id_b", [128, 128], BF16))
        one_b = E(nc.sbuf_tensor("one_b", [1, 128], BF16))
        one_f = E(nc.sbuf_tensor("one_f", [1, 128], F32))
        pb_sb = E(nc.sbuf_tensor("pb_sb", [44, 16], F32))
        pbe_sb = E(nc.sbuf_tensor("pbe_sb", [44, 16], BF16))
        indn_sb = E(nc.sbuf_tensor("indn_sb", [44, 512], BF16))
        indf_sb = E(nc.sbuf_tensor("indf_sb", [44, 64], BF16))
        ebn_sb = E(nc.sbuf_tensor("ebn_sb", [16, 512], BF16))
        ebf_sb = E(nc.sbuf_tensor("ebf_sb", [16, 64], BF16))
        wfar = E(nc.sbuf_tensor("wfar", [120, 768], BF16))
        ctxQK = ctx.enter_context(contextlib.ExitStack())
        q_lm = ctxQK.enter_context(nc.sbuf_tensor("q_lm", [128, 8 * 2048], BF16))
        k_lm = ctxQK.enter_context(nc.sbuf_tensor("k_lm", [128, 8 * 5120], BF16))
        ctx1 = contextlib.ExitStack()
        x_fm = ctx1.enter_context(nc.sbuf_tensor("x_fm", [128, 8 * 2048], BF16))
        xrot = ctx1.enter_context(nc.sbuf_tensor("xrot", [128, 2 * 1024], BF16))
        wrot = ctx1.enter_context(nc.sbuf_tensor("wrot", [128, 2 * 8 * 128], BF16))
        wvg = ctx1.enter_context(nc.sbuf_tensor("wvg", [128, 16 * 512], BF16))
        evrot = ctx1.enter_context(nc.sbuf_tensor("evrot", [128, 2 * 1040], BF16))
        grot = ctx1.enter_context(nc.sbuf_tensor("grot", [128, 4 * 512], BF16))
        bvrep = ctx1.enter_context(nc.sbuf_tensor("bvrep", [128, 1024], F32))
        rowsel = ctx1.enter_context(nc.sbuf_tensor("rowsel", [16, 16 * 128], BF16))
        brow_sb = ctx1.enter_context(nc.sbuf_tensor("brow_sb", [1, 1024], F32))
        reps_sb = ctx1.enter_context(nc.sbuf_tensor("reps_sb", [128, 512], BF16))

        PSF = [E(nc.psum_tensor("psf%d" % i, [128, 512], F32)) for i in range(6)]
        PSB = [E(nc.psum_tensor("psb%d" % i, [128, 1024], BF16)) for i in range(2)]

        sLD = E(nc.semaphore("sLD"))
        sXL = E(nc.semaphore("sXL"))
        sGD = E(nc.semaphore("sGD"))
        sT = E(nc.semaphore("sT"))
        sV = E(nc.semaphore("sV"))
        sA = E(nc.semaphore("sA"))
        sG = E(nc.semaphore("sG"))
        sST = E(nc.semaphore("sST"))
        block = E(nc.Block())

        n = dict(LD=0, XL=0, GD=0, T=0, V=0, A=0, G=0, ST=0)

        def iLD(i):
            n["LD"] += 16
            return i.then_inc(sLD, 16)

        def iXL(i):
            n["XL"] += 16
            return i.then_inc(sXL, 16)

        def iGD(i):
            n["GD"] += 16
            return i.then_inc(sGD, 16)

        def iT(i):
            n["T"] += 1
            return i.then_inc(sT, 1)

        def iV(i):
            n["V"] += 1
            return i.then_inc(sV, 1)

        def iA(i):
            n["A"] += 1
            return i.then_inc(sA, 1)

        def iG(i):
            n["G"] += 1
            return i.then_inc(sG, 1)

        def iST(i):
            n["ST"] += 16
            return i.then_inc(sST, 16)

        onT, onV, onA, onG, onS = block.tensor, block.vector, block.scalar, block.gpsimd, block.sync

        krow = k_lm[:, :].ap[0][0]
        qrow = q_lm[:, :].ap[0][0]
        xfrow = x_fm[:, :].ap[0][0]
        wfrow = wfar[:, :].ap[0][0]

        def sub(t_sb, off, dims):
            a = t_sb[:, :]
            return AP(a.tensor, a.offset + off, dims)

        def ph0(s):
            iLD(s.dma_start(out=id_b[:, :], in_=ident_b))
            iLD(s.dma_start(out=one_b[:, :], in_=ones_b))
            iLD(s.dma_start(out=one_f[:, :], in_=ones_f))
            iLD(s.dma_start(out=pb_sb[:, :], in_=pos_b))
            iLD(s.dma_start(out=indn_sb[:, :], in_=ind_n))
            iLD(s.dma_start(out=indf_sb[:, :], in_=ind_f))
            iLD(s.dma_start(out=b_qkv_sb[:, :], in_=b_qkv))
            iLD(s.dma_start(out=b_gate_sb[:, :], in_=b_gate))
            iLD(s.dma_start(out=brow_sb[:, :], in_=bv_row))
            iLD(s.dma_start(out=rowsel[:, :], in_=rowsel_in))
        onS(ph0)
        LD0 = n["LD"]

        def ph0v(v):
            v.wait_ge(sLD, LD0)
            v.tensor_scalar(b_qkv_sc[:, 0:8], b_qkv_sb[:, 0:8], 0.125, None, ALU.mult)
            iV(v.tensor_copy(b_qkv_sc[:, 8:24], b_qkv_sb[:, 8:24]))
        onV(ph0v)

        def ph0t(t):
            t.wait_ge(sLD, LD0)
            t.matmul(PSF[0][0:128, 0:512], one_f[0:1, :], brow_sb[0:1, 0:512], start=True, stop=True)
            iT(t.matmul(PSF[1][0:128, 0:512], one_f[0:1, :], brow_sb[0:1, 512:1024], start=True, stop=True))
        onT(ph0t)
        T_bv = n["T"]

        def ph0v2(v):
            v.wait_ge(sT, T_bv)
            v.tensor_copy(bvrep[:, 0:512], PSF[0][0:128, 0:512])
            iV(v.tensor_copy(bvrep[:, 512:1024], PSF[1][0:128, 0:512]))
        onV(ph0v2)
        V_bv = n["V"]

        def pb_exp(a):
            a.wait_ge(sLD, LD0)
            iA(a.activation(out=pbe_sb[:, :], in_=pb_sb[:, :], func=AF.Exp))
        onA(pb_exp)
        A_pb = n["A"]

        def eb_mm(t):
            t.wait_ge(sA, A_pb)
            t.wait_ge(sV, V_bv)
            t.matmul(PSF[0][0:16, 0:512], pbe_sb[:, :], indn_sb[:, :], start=True, stop=True)
            iT(t.matmul(PSF[1][0:16, 0:64], pbe_sb[:, :], indf_sb[:, :], start=True, stop=True))
        onT(eb_mm)
        T_eb = n["T"]

        def eb_ev(v):
            v.wait_ge(sT, T_eb)
            v.tensor_copy(ebn_sb[:, :], PSF[0][0:16, 0:512])
            iV(v.tensor_copy(ebf_sb[:, :], PSF[1][0:16, 0:64]))
        onV(eb_ev)

        for h in range(16):
            Vb = n["V"]

            def rn_t(t, h=h, Vb=Vb):
                t.wait_ge(sV, Vb)
                iT(t.matmul(PSF[2][0:128, 0:512], rowsel[:, h * 128:h * 128 + 128],
                            ebn_sb[:, 0:512], start=True, stop=True))
            onT(rn_t)
            Tb = n["T"]
            LDb = n["LD"]

            def rn_v(v, Tb=Tb, LDb=LDb):
                v.wait_ge(sT, Tb)
                v.wait_ge(sLD, LDb)
                iV(v.tensor_copy(reps_sb[:, 0:512], PSF[2][0:128, 0:512]))
            onV(rn_v)
            Vc = n["V"]

            def rn_s(s, h=h, Vc=Vc):
                s.wait_ge(sV, Vc)
                iLD(s.dma_start(out=repn_d.ap()[:, 512 * h:512 * h + 512], in_=reps_sb[:, 0:512]))
            onS(rn_s)

        for h in range(16):
            Vb = n["V"]

            def rf_t(t, h=h, Vb=Vb):
                t.wait_ge(sV, Vb)
                iT(t.matmul(PSF[2][0:128, 0:64], rowsel[:, h * 128:h * 128 + 128],
                            ebf_sb[:, 0:64], start=True, stop=True))
            onT(rf_t)
            Tb = n["T"]
            LDb = n["LD"]

            def rf_v(v, Tb=Tb, LDb=LDb):
                v.wait_ge(sT, Tb)
                v.wait_ge(sLD, LDb)
                iV(v.tensor_copy(reps_sb[:, 0:64], PSF[2][0:128, 0:64]))
            onV(rf_v)
            Vc = n["V"]

            def rf_s(s, h=h, Vc=Vc):
                s.wait_ge(sV, Vc)
                iLD(s.dma_start(out=repf_d.ap()[:, 64 * h:64 * h + 64], in_=reps_sb[:, 0:64]))
            onS(rf_s)
        LD_tabs = n["LD"]

        def wf_ms(v):
            iV(v.memset(wfar[:, :], 0.0))
        onV(wf_ms)
        V_wf = n["V"]

        def wf_dma(s):
            s.wait_ge(sLD, LD_tabs)
            s.wait_ge(sV, V_wf)
            for dlp in range(3):
                src = AP(repf_d, 40, [[1024 - 1, 40], [64, 16], [1, 16]])
                dst = AP(wfar[:, :].tensor,
                         wfar[:, :].offset + (40 * dlp) * wfrow + 16 * dlp,
                         [[wfrow, 40], [48, 16], [1, 16]])
                iLD(s.dma_start(out=dst, in_=src))
        onS(wf_dma)
        LD_wf = n["LD"]

        x1V = {}
        for tb in range(NB):
            Vb = n["V"]

            def x_load(g, tb=tb, Vb=Vb):
                if tb == 0:
                    g.wait_ge(sLD, LD0)
                if tb >= 2:
                    g.wait_ge(sV, x1V[tb - 2])
                iXL(g.dma_start(out=xrot[:, (tb % 2) * 1024:(tb % 2) * 1024 + 1024],
                                in_=x_in[tb * 128:tb * 128 + 128, :]))
            onG(x_load)
            XLc = n["XL"]

            def x_tr(t, tb=tb, XLc=XLc, Vb=Vb):
                t.wait_ge(sXL, XLc)
                if tb >= 2:
                    t.wait_ge(sV, x1V[tb - 2])
                for kc in range(8):
                    mm = t.transpose(PSB[tb % 2][0:128, kc * 128:kc * 128 + 128],
                                     xrot[:, (tb % 2) * 1024 + kc * 128:(tb % 2) * 1024 + kc * 128 + 128],
                                     id_b[:, :])
                    if kc == 7:
                        iT(mm)
            onT(x_tr)
            Tc = n["T"]

            def x_ev(v, tb=tb, Tc=Tc):
                v.wait_ge(sT, Tc)
                dst = sub(x_fm, tb * 128, [[xfrow, 128], [2048, 8], [1, 128]])
                iV(v.tensor_copy(dst, PSB[tb % 2][0:128, 0:1024]))
            onV(x_ev)
            x1V[tb] = n["V"]
        V_xfm = n["V"]

        def wsrc(f, kc):
            if f < 16:
                return w_qkv[kc * 128:kc * 128 + 128, f * 128:f * 128 + 128]
            return w_gate[kc * 128:kc * 128 + 128, (f - 16) * 128:(f - 16) * 128 + 128]

        A_ph2 = n["A"]
        gate_st = []
        for f in range(24):
            par = f % 2
            Ab = n["A"]

            def w_load(g, f=f, par=par, Ab=Ab):
                if f == 0:
                    g.wait_ge(sXL, 16 * NB)
                if f >= 2:
                    g.wait_ge(sA, Ab - 4)
                for kc in range(8):
                    iGD(g.dma_start(out=wrot[:, par * 1024 + kc * 128:par * 1024 + kc * 128 + 128],
                                    in_=wsrc(f, kc)))
            onG(w_load)
            GDc = n["GD"]
            Ab2 = n["A"]

            def qk_mm(t, f=f, par=par, GDc=GDc, Ab2=Ab2):
                t.wait_ge(sGD, GDc)
                if f == 0:
                    t.wait_ge(sV, V_xfm)
                if f >= 1:
                    t.wait_ge(sA, Ab2)
                for kc in range(8):
                    for tc in range(4):
                        mm = t.matmul(PSF[tc][0:128, 0:512],
                                      wrot[:, par * 1024 + kc * 128:par * 1024 + kc * 128 + 128],
                                      x_fm[:, kc * 2048 + tc * 512:kc * 2048 + tc * 512 + 512],
                                      start=(kc == 0), stop=(kc == 7))
                        if kc == 7:
                            iT(mm)
            onT(qk_mm)
            Tc = n["T"]

            def qk_ev(a, f=f, Tc=Tc):
                for tc in range(4):
                    a.wait_ge(sT, Tc - 4 + tc + 1)
                    if f < 8:
                        dst = sub(q_lm, f * 2048 + 4 * tc, [[qrow, 128], [1, 4], [16, 128]])
                        iA(a.activation(out=dst, in_=PSF[tc][0:128, 0:512], func=AF.Identity,
                                        bias=b_qkv_sc[:, f:f + 1], scale=0.125))
                    elif f < 16:
                        dst = sub(k_lm, (f - 8) * 5120 + 24 + 4 * tc, [[krow, 128], [1, 4], [40, 128]])
                        iA(a.activation(out=dst, in_=PSF[tc][0:128, 0:512], func=AF.Identity,
                                        bias=b_qkv_sb[:, f:f + 1], scale=1.0))
                    else:
                        gi = (f - 16) * 4 + tc
                        if gi >= 4:
                            a.wait_ge(sST, gate_st[gi - 4])
                        iA(a.activation(out=grot[:, (gi % 4) * 512:(gi % 4) * 512 + 512],
                                        in_=PSF[tc][0:128, 0:512], func=AF.Sigmoid,
                                        bias=b_gate_sb[:, f - 16:f - 16 + 1], scale=1.0))
            onA(qk_ev)

            if f >= 16:
                Ac = n["A"]

                def g_st(s, f=f, Ac=Ac):
                    for tc in range(4):
                        gi = (f - 16) * 4 + tc
                        s.wait_ge(sA, Ac - 4 + tc + 1)
                        iST(s.dma_start(out=gate_d.ap()[:, gi * 512:gi * 512 + 512],
                                        in_=grot[:, (gi % 4) * 512:(gi % 4) * 512 + 512]))
                        gate_st.append(n["ST"])
                onS(g_st)
        A_qkg = n["A"]

        def vw_load(g):
            for kc in range(8):
                for t2 in range(2):
                    iGD(g.dma_start(out=wvg[:, (kc * 2 + t2) * 512:(kc * 2 + t2) * 512 + 512],
                                    in_=w_qkv[kc * 128:kc * 128 + 128, 2048 + t2 * 512:2048 + t2 * 512 + 512]))
        onG(vw_load)
        GD_vw = n["GD"]

        def evr_init(v):
            v.wait_ge(sV, V_bv)
            iV(v.memset(evrot[:, :], 1.0))
        onV(evr_init)
        V_evr = n["V"]

        v_st = []
        vevV = {}
        for tb in range(NB):
            Vb = n["V"]

            def v_mm(t, tb=tb, Vb=Vb):
                if tb == 0:
                    t.wait_ge(sGD, GD_vw)
                    t.wait_ge(sA, A_qkg)
                if tb >= 1:
                    t.wait_ge(sV, vevV[tb - 1])
                for kc in range(8):
                    for t2 in range(2):
                        mm = t.matmul(PSF[4 + t2][0:128, 0:512],
                                      x_fm[:, kc * 2048 + tb * 128:kc * 2048 + tb * 128 + 128],
                                      wvg[:, (kc * 2 + t2) * 512:(kc * 2 + t2) * 512 + 512],
                                      start=(kc == 0), stop=(kc == 7))
                        if kc == 7:
                            iT(mm)
            onT(v_mm)
            Tc = n["T"]

            def v_ev(v, tb=tb, Tc=Tc):
                if tb == 0:
                    v.wait_ge(sV, V_evr)
                if tb >= 2:
                    v.wait_ge(sST, v_st[tb - 2])
                for t2 in range(2):
                    v.wait_ge(sT, Tc - 2 + t2 + 1)
                    dst = sub(evrot, (tb % 2) * 1040 + t2 * 8 * 65,
                              [[evrot[:, :].ap[0][0], 128], [65, 8], [1, 64]])
                    mm = v.scalar_tensor_tensor(dst, PSF[4 + t2][0:128, 0:512], 1.0,
                                                bvrep[:, t2 * 512:t2 * 512 + 512],
                                                ALU.mult, ALU.add)
                    if t2 == 1:
                        iV(mm)
            onV(v_ev)
            vevV[tb] = n["V"]
            Vc = n["V"]

            def v_stf(s, tb=tb, Vc=Vc):
                s.wait_ge(sV, Vc)
                iST(s.dma_start(out=vown_d.ap()[:, tb * 1040:tb * 1040 + 1040],
                                in_=evrot[:, (tb % 2) * 1040:(tb % 2) * 1040 + 1040]))
                v_st.append(n["ST"])
            onS(v_stf)

        ctx1.close()
        ctx2 = contextlib.ExitStack()
        v_tm = ctx2.enter_context(nc.sbuf_tensor("v_tm", [128, 40 * 1040], BF16))
        vrow = v_tm[:, :].ap[0][0]
        ctxK = contextlib.ExitStack()
        khrot = ctxK.enter_context(nc.sbuf_tensor("khrot", [128, 3072], BF16))

        kst_V = {}
        kst_ST = {}
        for hp in range(8):
            def ko_v(v, hp=hp):
                if hp == 0:
                    v.wait_ge(sA, A_qkg)
                if hp >= 1:
                    v.wait_ge(sST, kst_ST[hp - 1])
                srcv = sub(k_lm, hp * 5120 + 24, [[krow, 128], [1, 16], [40, 128]])
                iV(v.tensor_copy(khrot[:, 0:2048], srcv))
            onV(ko_v)
            kst_V[hp] = n["V"]
            Vc = n["V"]

            def ko_s(s, hp=hp, Vc=Vc):
                s.wait_ge(sV, Vc)
                iST(s.dma_start(out=kown_d.ap()[:, hp * 2048:hp * 2048 + 2048],
                                in_=khrot[:, 0:2048]))
                kst_ST[hp] = n["ST"]
            onS(ko_s)
        ST_kown = n["ST"]

        def halo_ms(v):
            v.wait_ge(sST, v_st[-1])
            iV(v.memset(v_tm[:, :], 0.0))
        onV(halo_ms)
        V_hms = n["V"]

        def coll(g):
            g.wait_ge(sST, ST_kown)
            g.wait_ge(sST, v_st[-1])
            iG(g.collective_compute("AllGather", ALU.bypass,
                                    replica_groups=[list(range(8))],
                                    ins=[kown_d.ap().opt()], outs=[kall_d.ap().opt()]))
            iG(g.collective_compute("AllGather", ALU.bypass,
                                    replica_groups=[list(range(8))],
                                    ins=[vown_d.ap().opt()], outs=[vall_d.ap().opt()]))
        onG(coll)
        G_coll = n["G"]

        GD_pre_v = n["GD"]

        pidbox = {}

        def v_pulls(g):
            g.wait_ge(sG, G_coll)
            g.wait_ge(sV, V_hms)
            pidbox["pid"] = g.partition_id()
            for c in nc.Switch(engines=[g], index=[pidbox["pid"]], n=8):
                cm1 = (c - 1) % 8
                cm2 = (c - 2) % 8
                p4 = c % 4
                src3 = AP(vall_d, cm1 * 128 * 16640, [[16640, 128], [1, 16640]])
                dst3 = sub(v_tm, 8 * 1040, [[vrow, 128], [1, 16640]])
                g.dma_start(out=dst3, in_=src3).then_inc(sGD, 16)
                src4 = AP(vall_d, cm2 * 128 * 16640 + 8 * 1040, [[16640, 128], [1, 8320]])
                dst4 = sub(v_tm, 0, [[vrow, 128], [1, 8320]])
                g.dma_start(out=dst4, in_=src4).then_inc(sGD, 16)
                srco = AP(vall_d, c * 128 * 16640, [[16640, 128], [1, 16640]])
                dsto = sub(v_tm, 24 * 1040, [[vrow, 128], [1, 16640]])
                g.dma_start(out=dsto, in_=srco).then_inc(sGD, 16)
                g.wait_ge(sGD, GD_pre_v + 3 * 16)
                if p4 == 0:
                    mv = sub(v_tm, 0, [[vrow, 128], [1, 24 * 1040]])
                elif p4 == 1:
                    mv = sub(v_tm, 0, [[vrow, 128], [1, 8 * 1040]])
                else:
                    prow = pb_sb[:, :].ap[0][0]
                    mv = AP(pb_sb[:, :].tensor, pb_sb[:, :].offset, [[prow, 1], [1, 8]])
                g.memset(mv, 0.0)
            n["GD"] += 3 * 16
            dstones = sub(v_tm, 64, [[vrow, 128], [1040, 40], [65, 16]])
            iG(g.memset(dstones, 1.0))
        onG(v_pulls)
        G_vh = n["G"]

        kh_V = {}
        for hp in range(8):
            GD_pre_k = n["GD"]

            def k_pull(g, hp=hp, GD_pre_k=GD_pre_k):
                if hp == 0:
                    g.wait_ge(sG, G_vh)
                if hp >= 1:
                    g.wait_ge(sV, kh_V[hp - 1])
                for c in nc.Switch(engines=[g], index=[pidbox["pid"]], n=8):
                    cm1 = (c - 1) % 8
                    cm2 = (c - 2) % 8
                    p4 = c % 4
                    s2 = AP(kall_d, cm2 * 128 * 16384 + hp * 2048 + 1024,
                            [[16384, 128], [1, 1024]])
                    g.dma_start(out=khrot[:, 0:1024], in_=s2).then_inc(sGD, 16)
                    s1 = AP(kall_d, cm1 * 128 * 16384 + hp * 2048,
                            [[16384, 128], [1, 2048]])
                    g.dma_start(out=khrot[:, 1024:3072], in_=s1).then_inc(sGD, 16)
                    g.wait_ge(sGD, GD_pre_k + 2 * 16)
                    if p4 == 0:
                        mk = khrot[:, 0:3072]
                    elif p4 == 1:
                        mk = khrot[:, 0:1024]
                    else:
                        prow = pb_sb[:, :].ap[0][0]
                        mk = AP(pb_sb[:, :].tensor, pb_sb[:, :].offset, [[prow, 1], [1, 8]])
                    g.memset(mk, 0.0)
                n["GD"] += 2 * 16
                iG(g.engine_nop())
            onG(k_pull)
            G_kp = n["G"]

            def k_resh(v, hp=hp, G_kp=G_kp):
                v.wait_ge(sG, G_kp)
                dstk = sub(k_lm, hp * 5120, [[krow, 128], [40, 128], [1, 24]])
                srck = AP(khrot[:, :].tensor, khrot[:, :].offset,
                          [[khrot[:, :].ap[0][0], 128], [1, 128], [128, 24]])
                iV(v.tensor_copy(dstk, srck))
            onV(k_resh)
            kh_V[hp] = n["V"]
        ctxK.close()
        G_halo_g = n["G"]
        GD_halo = n["GD"]

        def halo_fin(g):
            g.wait_ge(sV, kh_V[7])
            iG(g.engine_nop())
        onG(halo_fin)
        GD_halo = n["GD"]
        G_halo = n["G"]

        if DBG:
            def dbg_qkv(g):
                g.wait_ge(sG, G_halo)
                iGD(g.dma_start(out=dbg["d_q"], in_=q_lm[:, :]))
                iGD(g.dma_start(out=dbg["d_k"], in_=k_lm[:, :]))
                iGD(g.dma_start(out=dbg["d_v"], in_=v_tm[:, 0:4 * 1040]))
            onG(dbg_qkv)

        if STAGE == 2:
            def tail2(g):
                g.wait_ge(sG, G_halo)
                dstt = AP(y_out.tensor, 0, [[16384, 128], [1, 16384]])
                iGD(g.dma_start(out=dstt, in_=q_lm[:, :]))
                g.wait_ge(sGD, n["GD"])
                g.wait_ge(sST, n["ST"])
            onG(tail2)
            ctx2.close()
            return nc, dbg

        ctxF = contextlib.ExitStack()
        efar = ctxF.enter_context(nc.sbuf_tensor("efar", [120, 1024], BF16))
        afar = ctxF.enter_context(nc.sbuf_tensor("afar", [120, 2 * 768], BF16))
        vsub = ctxF.enter_context(nc.sbuf_tensor("vsub", [120, 1040], BF16))
        gfrot = ctxF.enter_context(nc.sbuf_tensor("gfrot", [65, 768], BF16))
        far_Tav = {}
        far_Aexp = {}
        far_Vmul = {}
        far_Vev = {}
        far_ST = {}
        for li, (l0, nl) in enumerate(LTS):
            npart = 40 * nl
            qw = 16 * nl
            par = li % 2

            def vs_pull(g, l0=l0, nl=nl, par=par, li=li):
                if li == 0:
                    g.wait_ge(sG, G_halo)
                if li >= 1:
                    g.wait_ge(sT, far_Tav[li - 1])
                src = AP(v_tm[:, :].tensor, v_tm[:, :].offset + l0 * vrow,
                         [[vrow, nl], [1040, 40], [1, 1040]])
                iGD(g.dma_start(out=vsub[0:40 * nl, 0:1040], in_=src))
            onG(vs_pull)
            GD_vs = n["GD"]
            if FAR_PARTS < 2:
                continue

            def f_sc(t, l0=l0, nl=nl, par=par, li=li, npart=npart, qw=qw):
                if li == 0:
                    t.wait_ge(sGD, GD_halo)
                    t.wait_ge(sV, kh_V[7])
                    t.wait_ge(sLD, LD_wf)
                if li >= 2:
                    t.wait_ge(sA, far_Aexp[li - 2])
                for h in range(16):
                    half = 64 * (h % 2)
                    hp = h // 2
                    stat = AP(k_lm[:, :].tensor,
                              k_lm[:, :].offset + half * krow + hp * 5120 + l0 * 40,
                              [[krow, 64], [1, npart]])
                    mov = AP(q_lm[:, :].tensor,
                             q_lm[:, :].offset + half * qrow + hp * 2048 + l0 * 16,
                             [[qrow, 64], [1, qw]])
                    bk = 2 * par + (h % 2)
                    mm = t.matmul(PSF[bk][0:npart, (h // 2) * 64:(h // 2) * 64 + qw],
                                  stat, mov, start=True, stop=True)
                    if h == 15:
                        iT(mm)
            onT(f_sc)
            T_fsc = n["T"]
            if FAR_PARTS < 3:
                continue

            def f_exp(a, par=par, li=li, npart=npart, T_fsc=T_fsc):
                a.wait_ge(sT, T_fsc)
                if li >= 1:
                    a.wait_ge(sV, far_Vmul[li - 1])
                for b2 in range(2):
                    mm = a.activation(out=efar[0:npart, b2 * 512:b2 * 512 + 512],
                                      in_=PSF[2 * par + b2][0:npart, 0:512], func=AF.Exp)
                    if b2 == 1:
                        iA(mm)
            onA(f_exp)
            far_Aexp[li] = n["A"]
            A_fexp = n["A"]
            if FAR_PARTS < 4:
                continue

            def f_mul(v, par=par, li=li, npart=npart, qw=qw, A_fexp=A_fexp):
                v.wait_ge(sA, A_fexp)
                if li >= 2:
                    v.wait_ge(sT, far_Tav[li - 2])
                ef = efar[:, :]
                src0 = AP(ef.tensor, ef.offset,
                          [[ef.ap[0][0], npart], [512, 2], [64, 8], [1, qw]])
                src1 = AP(wfar[:, :].tensor, wfar[:, :].offset,
                          [[wfrow, npart], [48, 2], [96, 8], [1, qw]])
                dst = AP(afar[:, :].tensor, afar[:, :].offset + par * 768,
                         [[afar[:, :].ap[0][0], npart], [48, 2], [96, 8], [1, qw]])
                iV(v.tensor_tensor(dst, src0, src1, ALU.mult))
            onV(f_mul)
            far_Vmul[li] = n["V"]
            V_fmul = n["V"]
            if FAR_PARTS < 5:
                continue

            def f_av(t, par=par, li=li, npart=npart, qw=qw, V_fmul=V_fmul, GD_vs=GD_vs):
                t.wait_ge(sV, V_fmul)
                t.wait_ge(sGD, GD_vs)
                if li >= 1:
                    t.wait_ge(sV, far_Vev[li - 1])
                for h in range(16):
                    stat = AP(vsub[:, :].tensor,
                              vsub[:, :].offset + h * 65,
                              [[vsub[:, :].ap[0][0], npart], [1, 65]])
                    mov = AP(afar[:, :].tensor, afar[:, :].offset + par * 768 + h * 48,
                             [[afar[:, :].ap[0][0], npart], [1, qw]])
                    bk = 4 + (h // 8)
                    mm = t.matmul(PSF[bk][0:65, (h % 8) * 64:(h % 8) * 64 + qw],
                                  stat, mov, start=True, stop=True)
                    if h == 15:
                        iT(mm)
            onT(f_av)
            far_Tav[li] = n["T"]
            T_fav = n["T"]
            if FAR_PARTS < 6:
                continue

            def f_ev(v, par=par, li=li, qw=qw, T_fav=T_fav):
                v.wait_ge(sT, T_fav)
                if li >= 1:
                    v.wait_ge(sST, far_ST[li - 1])
                for b2 in range(2):
                    pt = PSF[4 + b2][0:65, 0:512]
                    src = AP(pt.tensor, pt.offset, [[512, 65], [64, 8], [1, qw]])
                    dst = AP(gfrot[:, :].tensor, gfrot[:, :].offset + b2 * 384,
                             [[gfrot[:, :].ap[0][0], 65], [48, 8], [1, qw]])
                    mm = v.tensor_copy(dst, src)
                    if b2 == 1:
                        iV(mm)
            onV(f_ev)
            far_Vev[li] = n["V"]
            V_fev = n["V"]
            if FAR_PARTS < 7:
                continue

            def f_st(s, par=par, li=li, l0=l0, qw=qw, V_fev=V_fev):
                s.wait_ge(sV, V_fev)
                src = AP(gfrot[:, :].tensor, gfrot[:, :].offset,
                         [[gfrot[:, :].ap[0][0], 65], [48, 16], [1, qw]])
                dst = AP(gfar_d, l0 * 16,
                         [[2048, 65], [65 * 2048, 16], [1, qw]])
                iST(s.dma_start(out=dst, in_=src))
                far_ST[li] = n["ST"]
            onS(f_st)
        ST_far_all = n["ST"]

        if STAGE == 4:
            def tail4(g):
                g.wait_ge(sST, ST_far_all)
                dstt = AP(y_out.tensor, 0, [[16384, 128], [1, 16384]])
                iGD(g.dma_start(out=dstt, in_=q_lm[:, :]))
                if DBG and DBG_GN4:
                    for hd, dname in ((0, "d_gn"), (15, "d_gf15")):
                        for ch in range(2):
                            iGD(g.dma_start(out=efar[0:65, 0:1024],
                                            in_=gfar_d.ap()[hd, 0:65, ch * 1024:ch * 1024 + 1024]))
                            g.wait_ge(sGD, n["GD"])
                            iGD(g.dma_start(out=dbg[dname][0:65, ch * 1024:ch * 1024 + 1024],
                                            in_=efar[0:65, 0:1024]))
                            g.wait_ge(sGD, n["GD"])
                g.wait_ge(sGD, n["GD"])
                g.wait_ge(sST, n["ST"])
            onG(tail4)
            ctxF.close()
            ctx2.close()
            return nc, dbg

        ctxF.close()
        ctxN = contextlib.ExitStack()
        enear = ctxN.enter_context(nc.sbuf_tensor("enear", [128, 2 * 384], BF16))
        anear = ctxN.enter_context(nc.sbuf_tensor("anear", [128, 2 * 384], BF16))
        wnear = ctxN.enter_context(nc.sbuf_tensor("wnear", [128, 384], BF16))
        gnrot = ctxN.enter_context(nc.sbuf_tensor("gnrot", [65, 4 * 128], BF16))
        gfh = ctxN.enter_context(nc.sbuf_tensor("gfh", [65, 2048], BF16))
        near_VW = {}
        near_Vgf = {}
        near_Aexp = {}
        near_Vmul = {}
        near_Tav = {}
        near_Vev = {}
        near_STd = {}
        for h in range(16):
            half = 64 * (h % 2)
            hp = h // 2
            par_h = h % 2

            def n_pull(s, h=h, par_h=par_h):
                if h == 0:
                    s.wait_ge(sLD, LD_wf)
                    s.wait_ge(sST, ST_far_all)
                if h >= 1:
                    s.wait_ge(sV, near_VW[h - 1])
                src = AP(repn_d, 512 * h + 128, [[8192 - 1, 128], [1, 384]])
                iLD(s.dma_start(out=wnear[:, 0:384], in_=src))
                if h >= 1:
                    s.wait_ge(sV, near_Vgf[h - 1])
                iLD(s.dma_start(out=gfh[:, :], in_=gfar_d.ap()[h, 0:65, 0:2048]))
            onS(n_pull)
            LD_np = n["LD"]

            h_first = {}
            for mi, m in enumerate(range(-2, 16)):
                kb = m + 24
                jlist = [j for j in range(3) if 0 <= m + j < 16]
                sbank = mi % 2

                gi = h * 18 + mi

                def n_sc(t, h=h, half=half, hp=hp, m=m, kb=kb, jlist=jlist,
                         sbank=sbank, mi=mi, LD_np=LD_np, gi=gi):
                    if m == -2:
                        t.wait_ge(sLD, LD_np)
                    if gi >= 2:
                        pk = divmod(gi - 2, 18)
                        t.wait_ge(sA, near_Aexp[pk])
                    stat = AP(k_lm[:, :].tensor,
                              k_lm[:, :].offset + half * krow + hp * 5120 + kb,
                              [[krow, 64], [40, 128]])
                    last = None
                    for j in jlist:
                        mov = AP(q_lm[:, :].tensor,
                                 q_lm[:, :].offset + half * qrow + hp * 2048 + (m + j),
                                 [[qrow, 64], [16, 128]])
                        last = t.matmul(PSF[sbank][0:128, j * 128:j * 128 + 128],
                                        stat, mov, start=True, stop=True)
                    iT(last)
                onT(n_sc)
                T_nsc = n["T"]

                def n_exp(a, sbank=sbank, jlist=jlist, T_nsc=T_nsc, h=h, mi=mi, gi=gi):
                    a.wait_ge(sT, T_nsc)
                    if gi >= 2:
                        a.wait_ge(sV, near_Vmul[divmod(gi - 2, 18)])
                    j0, j1 = jlist[0], jlist[-1]
                    iA(a.activation(out=enear[:, (mi % 2) * 384 + j0 * 128:(mi % 2) * 384 + (j1 + 1) * 128],
                                    in_=PSF[sbank][0:128, j0 * 128:(j1 + 1) * 128], func=AF.Exp))
                onA(n_exp)
                near_Aexp[(h, mi)] = n["A"]
                A_nexp = n["A"]

                def n_mul(v, mi=mi, jlist=jlist, par_h=par_h, A_nexp=A_nexp, h=h, gi=gi):
                    v.wait_ge(sA, A_nexp)
                    if gi >= 2:
                        v.wait_ge(sT, near_Tav[divmod(gi - 2, 18)])
                    j0, j1 = jlist[0], jlist[-1]
                    w = (j1 + 1 - j0) * 128
                    iV(v.tensor_tensor(anear[:, (mi % 2) * 384 + j0 * 128:(mi % 2) * 384 + j0 * 128 + w],
                                       enear[:, (mi % 2) * 384 + j0 * 128:(mi % 2) * 384 + j0 * 128 + w],
                                       wnear[:, j0 * 128:j0 * 128 + w],
                                       ALU.mult))
                onV(n_mul)
                near_Vmul[(h, mi)] = n["V"]
                V_nmul = n["V"]

                def n_av(t, h=h, m=m, kb=kb, jlist=jlist, mi=mi, V_nmul=V_nmul):
                    t.wait_ge(sV, V_nmul)
                    stat = sub(v_tm, kb * 1040 + h * 65, [[vrow, 128], [1, 65]])
                    last = None
                    for j in jlist:
                        Bq = m + j
                        bk = 2 + (Bq % 4)
                        key = (h, Bq)
                        first = key not in h_first
                        if first and (h, Bq - 4) in near_Vev:
                            t.wait_ge(sV, near_Vev[(h, Bq - 4)])
                        if first and Bq < 4 and (h - 1, Bq + 12) in near_Vev:
                            t.wait_ge(sV, near_Vev[(h - 1, Bq + 12)])
                        mov = AP(anear[:, :].tensor,
                                 anear[:, :].offset + (mi % 2) * 384 + j * 128,
                                 [[anear[:, :].ap[0][0], 128], [1, 128]])
                        last = t.matmul(PSF[bk][0:65, 0:128], stat, mov,
                                        start=first, stop=(j == 0))
                        h_first[key] = True
                    iT(last)
                onT(n_av)
                near_Tav[(h, mi)] = n["T"]
                T_nav = n["T"]

                if m >= 0:
                    Bq = m

                    def n_ev(v, h=h, Bq=Bq, T_nav=T_nav):
                        v.wait_ge(sT, T_nav)
                        idx = h * 16 + Bq - 4
                        if idx >= 0:
                            hh, bb = divmod(idx, 16)
                            v.wait_ge(sST, near_STd[(hh, bb)])
                        bk = 2 + (Bq % 4)
                        gsl = AP(gfh[:, :].tensor, gfh[:, :].offset + Bq,
                                 [[gfh[:, :].ap[0][0], 65], [16, 128]])
                        iV(v.scalar_tensor_tensor(gnrot[:, (Bq % 4) * 128:(Bq % 4) * 128 + 128],
                                                  PSF[bk][0:65, 0:128], 1.0, gsl,
                                                  ALU.mult, ALU.add))
                    onV(n_ev)
                    near_Vev[(h, Bq)] = n["V"]
                    Vev = n["V"]

                    def n_st(s, h=h, Bq=Bq, Vev=Vev):
                        s.wait_ge(sV, Vev)
                        iST(s.dma_start(out=gnear_d.ap()[h, 0:65, Bq * 128:Bq * 128 + 128],
                                        in_=gnrot[:, (Bq % 4) * 128:(Bq % 4) * 128 + 128]))
                        near_STd[(h, Bq)] = n["ST"]
                    onS(n_st)
            near_VW[h] = near_Vmul[(h, 17)]
            near_Vgf[h] = near_Vev[(h, 15)]
        ST_near_all = n["ST"]

        if DBG:
            def dbg_gn_pull(s):
                s.wait_ge(sST, ST_near_all)
                s.wait_ge(sV, near_Vgf[15])
                iLD(s.dma_start(out=gfh[:, :], in_=gnear_d.ap()[0, 0:65, 0:2048]))
            onS(dbg_gn_pull)
            LD_dgn = n["LD"]

            def dbg_gn(g):
                g.wait_ge(sLD, LD_dgn)
                iGD(g.dma_start(out=dbg["d_gn"], in_=gfh[:, :]))
            onG(dbg_gn)

        if STAGE == 5:
            def tail5(g):
                g.wait_ge(sST, ST_near_all)
                dstt = AP(y_out.tensor, 0, [[16384, 128], [1, 16384]])
                iGD(g.dma_start(out=dstt, in_=q_lm[:, :]))
                g.wait_ge(sGD, n["GD"])
                g.wait_ge(sST, n["ST"])
            onG(tail5)
            ctxN.close()
            ctx2.close()
            return nc, dbg

        ctxN.close()
        ctx2.close()
        ctxQK.close()
        ctx3 = contextlib.ExitStack()
        x1_sb = ctx3.enter_context(nc.sbuf_tensor("x1_sb", [128, 16 * 1024], F32))
        stat_sb = ctx3.enter_context(nc.sbuf_tensor("stat_sb", [128, 8], F32))
        stat2_sb = ctx3.enter_context(nc.sbuf_tensor("stat2_sb", [128, 8], F32))
        stat3_sb = ctx3.enter_context(nc.sbuf_tensor("stat3_sb", [128, 8], F32))
        mu_sb = ctx3.enter_context(nc.sbuf_tensor("mu_sb", [128, 8], F32))
        tmp_sb = ctx3.enter_context(nc.sbuf_tensor("tmp_sb", [128, 8], F32))
        nms_sb = ctx3.enter_context(nc.sbuf_tensor("nms_sb", [128, 8], F32))
        ctx3T = contextlib.ExitStack()
        pfm = ctx3T.enter_context(nc.sbuf_tensor("pfm", [128, 8 * 2048], BF16))
        wvg2 = ctx3T.enter_context(nc.sbuf_tensor("wvg2", [128, 16 * 512], BF16))
        zall = ctx3T.enter_context(nc.sbuf_tensor("zall", [16, 2048], BF16))
        sel2 = ctx3T.enter_context(nc.sbuf_tensor("sel2", [16, 8 * 128], BF16))
        zrec = ctx3T.enter_context(nc.sbuf_tensor("zrec", [128, 2048], F32))
        gaterot = ctx3T.enter_context(nc.sbuf_tensor("gaterot", [128, 2048], BF16))
        xf32rot = ctx3T.enter_context(nc.sbuf_tensor("xf32rot", [128, 2 * 1024], F32))
        bvrep2 = ctx3T.enter_context(nc.sbuf_tensor("bvrep2", [128, 1024], F32))
        brow3 = ctx3T.enter_context(nc.sbuf_tensor("brow3", [1, 1024], F32))
        pfrow = pfm[:, :].ap[0][0]

        def p6_pull(s):
            s.wait_ge(sST, ST_near_all)
            iLD(s.dma_start(out=brow3[:, :], in_=b_out))
            iLD(s.dma_start(out=sel2[:, :], in_=sel2_in))
            iLD(s.dma_start(out=zall[:, :],
                            in_=AP(gnear_d, 64 * 2048, [[65 * 2048, 16], [1, 2048]])))
        onS(p6_pull)
        LD_p6 = n["LD"]

        def p6_wo(g):
            g.wait_ge(sST, ST_near_all)
            for kc in range(8):
                for t2 in range(2):
                    iGD(g.dma_start(out=wvg2[:, (kc * 2 + t2) * 512:(kc * 2 + t2) * 512 + 512],
                                    in_=w_out_d[kc * 128:kc * 128 + 128, t2 * 512:t2 * 512 + 512]))
        onG(p6_wo)
        GD_p6 = n["GD"]

        def p6_borep(t):
            t.wait_ge(sLD, LD_p6)
            t.wait_ge(sV, near_Vev[(15, 15)])
            t.matmul(PSF[0][0:128, 0:512], one_f[0:1, :], brow3[0:1, 0:512], start=True, stop=True)
            iT(t.matmul(PSF[1][0:128, 0:512], one_f[0:1, :], brow3[0:1, 512:1024], start=True, stop=True))
        onT(p6_borep)
        T_bo = n["T"]

        def p6_borep_ev(v):
            v.wait_ge(sT, T_bo)
            v.tensor_copy(bvrep2[:, 0:512], PSF[0][0:128, 0:512])
            iV(v.tensor_copy(bvrep2[:, 512:1024], PSF[1][0:128, 0:512]))
        onV(p6_borep_ev)
        V_bo = n["V"]

        pV_mul = {}
        for hp2 in range(8):
            def p_pull(s, hp2=hp2):
                if hp2 == 0:
                    s.wait_ge(sV, V_bo)
                if hp2 >= 1:
                    s.wait_ge(sV, pV_mul[hp2 - 1])
                h0, h1 = 2 * hp2, 2 * hp2 + 1
                dst0 = sub(pfm, hp2 * 2048, [[pfrow, 64], [1, 2048]])
                iLD(s.dma_start(out=dst0, in_=gnear_d.ap()[h0, 0:64, 0:2048]))
                dst1 = sub(pfm, 64 * pfrow + hp2 * 2048, [[pfrow, 64], [1, 2048]])
                iLD(s.dma_start(out=dst1, in_=gnear_d.ap()[h1, 0:64, 0:2048]))
                iLD(s.dma_start(out=gaterot[:, :], in_=gate_d.ap()[:, hp2 * 2048:hp2 * 2048 + 2048]))
            onS(p_pull)
            LD_pp = n["LD"]

            def p_zrep(t, hp2=hp2, LD_pp=LD_pp):
                t.wait_ge(sLD, LD_pp)
                if hp2 >= 1:
                    t.wait_ge(sV, pV_mul[hp2 - 1])
                for tc in range(4):
                    iT(t.matmul(PSF[2 + tc][0:128, 0:512], sel2[:, hp2 * 128:hp2 * 128 + 128],
                                zall[:, tc * 512:tc * 512 + 512], start=True, stop=True))
            onT(p_zrep)
            T_zr = n["T"]

            def p_mul(v, hp2=hp2, T_zr=T_zr, LD_pp=LD_pp):
                v.wait_ge(sT, T_zr)
                v.wait_ge(sLD, LD_pp)
                for tc in range(4):
                    v.reciprocal(zrec[:, tc * 512:tc * 512 + 512], PSF[2 + tc][0:128, 0:512])
                v.tensor_tensor(pfm[:, hp2 * 2048:hp2 * 2048 + 2048],
                                pfm[:, hp2 * 2048:hp2 * 2048 + 2048],
                                gaterot[:, :], ALU.mult)
                iV(v.tensor_tensor(pfm[:, hp2 * 2048:hp2 * 2048 + 2048],
                                   pfm[:, hp2 * 2048:hp2 * 2048 + 2048],
                                   zrec[:, :], ALU.mult))
            onV(p_mul)
            pV_mul[hp2] = n["V"]
        V_pfm = n["V"]

        if DBG:
            def dbg_pf(g):
                g.wait_ge(sV, V_pfm)
                iGD(g.dma_start(out=dbg["d_pf"], in_=pfm[:, :]))
                iGD(g.dma_start(out=dbg["d_za"], in_=zall[:, :]))
                iGD(g.dma_start(out=dbg["d_g15"], in_=gnear_d.ap()[15, 0:65, 0:2048]))
                iGD(g.dma_start(out=dbg["d_gf15"], in_=gfar_d.ap()[15, 0:65, 0:2048]))
            onG(dbg_pf)
        GD_dbgpf = n["GD"]

        wo_V = {}
        for tb in range(NB):
            def xr_load(g, tb=tb):
                if tb == 0:
                    g.wait_ge(sV, V_pfm)
                if tb >= 2:
                    g.wait_ge(sV, wo_V[tb - 2])
                iGD(g.dma_start(out=xf32rot[:, (tb % 2) * 1024:(tb % 2) * 1024 + 1024],
                                in_=x_in[tb * 128:tb * 128 + 128, :]))
            onG(xr_load)
            GD_xr = n["GD"]

            def wo_mm(t, tb=tb):
                if tb <= 1:
                    t.wait_ge(sV, V_pfm)
                if tb == 0:
                    t.wait_ge(sLD, LD_p6)
                    t.wait_ge(sGD, GD_p6)
                if tb >= 2:
                    t.wait_ge(sV, wo_V[tb - 2])
                bb = (tb % 2) * 2
                for kc in range(8):
                    for t2 in range(2):
                        mm = t.matmul(PSF[bb + t2][0:128, 0:512],
                                      pfm[:, kc * 2048 + tb * 128:kc * 2048 + tb * 128 + 128],
                                      wvg2[:, (kc * 2 + t2) * 512:(kc * 2 + t2) * 512 + 512],
                                      start=(kc == 0), stop=(kc == 7))
                        if kc == 7:
                            iT(mm)
            onT(wo_mm)
            T_wo = n["T"]

            def wo_ev(v, tb=tb, T_wo=T_wo, GD_xr=GD_xr):
                v.wait_ge(sT, T_wo)
                v.wait_ge(sGD, GD_xr)
                bb = (tb % 2) * 2
                for t2 in range(2):
                    v.scalar_tensor_tensor(x1_sb[:, tb * 1024 + t2 * 512:tb * 1024 + t2 * 512 + 512],
                                           PSF[bb + t2][0:128, 0:512], 1.0,
                                           bvrep2[:, t2 * 512:t2 * 512 + 512], ALU.mult, ALU.add)
                    mm = v.tensor_tensor(x1_sb[:, tb * 1024 + t2 * 512:tb * 1024 + t2 * 512 + 512],
                                         x1_sb[:, tb * 1024 + t2 * 512:tb * 1024 + t2 * 512 + 512],
                                         xf32rot[:, (tb % 2) * 1024 + t2 * 512:(tb % 2) * 1024 + t2 * 512 + 512],
                                         ALU.add)
                    if t2 == 1:
                        iV(mm)
            onV(wo_ev)
            wo_V[tb] = n["V"]

        ln1_V = {}
        for tb in range(NB if LN1_ON else 0):
            c8 = tb % 8

            def ln_a1(a, tb=tb, c8=c8):
                a.wait_ge(sV, wo_V[tb])
                if tb >= 8:
                    a.wait_ge(sV, ln1_V[tb - 8])
                a.activation(out=zrec[:, 0:1024], in_=x1_sb[:, tb * 1024:tb * 1024 + 1024],
                             func=AF.Identity, accum_out=stat_sb[:, c8:c8 + 1])
                a.activation(out=zrec[:, 1024:2048], in_=x1_sb[:, tb * 1024:tb * 1024 + 1024],
                             func=AF.Square, accum_out=stat2_sb[:, c8:c8 + 1])
                iA(a.activation(out=zrec[:, 0:1], in_=stat2_sb[:, c8:c8 + 1],
                                func=AF.Identity))
            onA(ln_a1)
            A_ln = n["A"]

            def ln_v1(v, c8=c8, A_ln=A_ln):
                v.wait_ge(sA, A_ln)
                v.tensor_scalar(mu_sb[:, c8:c8 + 1], stat_sb[:, c8:c8 + 1],
                                -1.0 / 1024.0, None, ALU.mult)
                v.tensor_scalar(stat3_sb[:, c8:c8 + 1], stat2_sb[:, c8:c8 + 1],
                                1.0 / 1024.0, LN_EPS_1, ALU.mult, ALU.add)
                v.tensor_copy(zrec[:, 1024:1152], zrec[:, 0:128])
                v.tensor_tensor(tmp_sb[:, c8:c8 + 1], mu_sb[:, c8:c8 + 1],
                                mu_sb[:, c8:c8 + 1], ALU.mult)
                v.tensor_copy(zrec[:, 1024:1152], zrec[:, 0:128])
                iV(v.tensor_tensor(nms_sb[:, c8:c8 + 1], stat3_sb[:, c8:c8 + 1],
                                   tmp_sb[:, c8:c8 + 1], ALU.subtract))
            onV(ln_v1)
            V_ln = n["V"]

            def ln_a2(a, c8=c8, V_ln=V_ln):
                a.wait_ge(sV, V_ln)
                iA(a.activation(out=tmp_sb[:, c8:c8 + 1], in_=nms_sb[:, c8:c8 + 1],
                                func=AF.Sqrt))
            onA(ln_a2)
            A_ln2 = n["A"]

            def ln_v2(v, c8=c8, A_ln2=A_ln2):
                v.wait_ge(sA, A_ln2)
                v.reciprocal(stat3_sb[:, c8:c8 + 1], tmp_sb[:, c8:c8 + 1])
                v.tensor_copy(zrec[:, 1024:1152], zrec[:, 0:128])
                iV(v.tensor_tensor(nms_sb[:, c8:c8 + 1], mu_sb[:, c8:c8 + 1],
                                   stat3_sb[:, c8:c8 + 1], ALU.mult))
            onV(ln_v2)
            V_ln2 = n["V"]

            def ln_a3(a, tb=tb, c8=c8, V_ln2=V_ln2):
                a.wait_ge(sV, V_ln2)
                iA(a.activation(out=x1_sb[:, tb * 1024:tb * 1024 + 1024],
                                in_=x1_sb[:, tb * 1024:tb * 1024 + 1024], func=AF.Identity,
                                scale=stat3_sb[:, c8:c8 + 1], bias=nms_sb[:, c8:c8 + 1]))
            onA(ln_a3)
            ln1_V[tb] = n["V"]
            near_dummy = None
        A_ln1 = n["A"]

        if DBG:
            def dbg_x1(s):
                s.wait_ge(sA, A_ln1)
                s.wait_ge(sV, wo_V[NB - 1])
                if ln1_V:
                    s.wait_ge(sV, ln1_V[NB - 1])
                for tb in range(NB):
                    iST(s.dma_start(out=dbg["d_x1"][tb * 128:tb * 128 + 128, :],
                                    in_=x1_sb[:, tb * 1024:tb * 1024 + 1024]))
                iST(s.dma_start(out=dbg["d_st"][0:128, 0:8], in_=stat_sb[:, :]))
                iST(s.dma_start(out=dbg["d_st"][0:128, 8:16], in_=stat2_sb[:, :]))
                iST(s.dma_start(out=dbg["d_st"][0:128, 16:24], in_=stat3_sb[:, :]))
                iST(s.dma_start(out=dbg["d_st"][0:128, 24:32], in_=mu_sb[:, :]))
            onS(dbg_x1)

        ST_dbg_x1 = n["ST"]

        if STAGE < 9:
            def out_st(s):
                s.wait_ge(sA, A_ln1)
                s.wait_ge(sV, wo_V[NB - 1])
                for tb in range(NB):
                    iST(s.dma_start(out=y_out[tb * 128:tb * 128 + 128, :],
                                    in_=x1_sb[:, tb * 1024:tb * 1024 + 1024]))
                s.wait_ge(sST, n["ST"])
            onS(out_st)
            ctx3T.close()
            ctx3.close()
            return nc, dbg

        ctx3T.close()
        ctxF2 = contextlib.ExitStack()
        xn_sb = ctxF2.enter_context(nc.sbuf_tensor("xn_sb", [128, 8 * 2048], BF16))
        wq_sb = ctxF2.enter_context(nc.sbuf_tensor("wq_sb", [128, 16 * 512], BF16))
        wrot2 = ctxF2.enter_context(nc.sbuf_tensor("wrot2", [128, 2 * 1024], F32))
        scrf = ctxF2.enter_context(nc.sbuf_tensor("scrf", [128, 2048], F32))
        xnb = ctxF2.enter_context(nc.sbuf_tensor("xnb", [128, 2 * 1024], BF16))
        sgnb = ctxF2.enter_context(nc.sbuf_tensor("sgnb", [128, 1024], BF16))
        garep = ctxF2.enter_context(nc.sbuf_tensor("garep", [128, 8], F32))
        berep = ctxF2.enter_context(nc.sbuf_tensor("berep", [128, 8], F32))
        wsum = ctxF2.enter_context(nc.sbuf_tensor("wsum", [128, 8], F32))
        mxt = ctxF2.enter_context(nc.sbuf_tensor("mxt", [128, 8], F32))
        onecol = ctxF2.enter_context(nc.sbuf_tensor("onecol", [128, 1], F32))
        gmx_b = ctxF2.enter_context(nc.sbuf_tensor("gmx_b", [128, 1], BF16))
        frep = ctxF2.enter_context(nc.sbuf_tensor("frep", [128, 8], F32))
        fsc = ctxF2.enter_context(nc.sbuf_tensor("fsc", [1, 8], F32))
        gmrow = ctxF2.enter_context(nc.sbuf_tensor("gmrow", [1, 8192], F32))
        xnrow = xn_sb[:, :].ap[0][0]

        def vpad(v):
            v.tensor_copy(scrf[:, 128:256], scrf[:, 0:128])

        def f0_pull(s):
            s.wait_ge(sA, A_ln1)
            s.wait_ge(sST, ST_dbg_x1)
            iLD(s.dma_start(out=garep[:, :], in_=g_ffn))
            iLD(s.dma_start(out=berep[:, :], in_=be_ffn))
        onS(f0_pull)
        LD_f0 = n["LD"]

        def f0_ms(v):
            v.wait_ge(sA, A_ln1)
            v.wait_ge(sV, wo_V[NB - 1])
            iV(v.memset(onecol[:, :], 1.0))
        onV(f0_ms)
        V_onec = n["V"]

        wabs_A = {}
        for wc in range(8):
            def w1_ld(g, wc=wc):
                if wc == 0:
                    g.wait_ge(sST, ST_dbg_x1)
                    g.wait_ge(sGD, GD_dbgpf)
                    g.wait_ge(sA, A_ln1)
                if wc >= 2:
                    g.wait_ge(sA, wabs_A[wc - 2])
                iGD(g.dma_start(out=wrot2[:, (wc % 2) * 1024:(wc % 2) * 1024 + 1024],
                                in_=w_ffn[wc * 128:wc * 128 + 128, :]))
            onG(w1_ld)
            GDc = n["GD"]

            def w1_abs(a, wc=wc, GDc=GDc):
                a.wait_ge(sGD, GDc)
                a.activation(out=scrf[:, 0:1024], in_=wrot2[:, (wc % 2) * 1024:(wc % 2) * 1024 + 1024],
                             func=AF.Abs, accum_out=wsum[:, wc:wc + 1])
                iA(a.activation(out=scrf[:, 0:1], in_=wsum[:, wc:wc + 1], func=AF.Identity))
            onA(w1_abs)
            wabs_A[wc] = n["A"]
        A_wabs = n["A"]

        def f1_mm(t):
            t.wait_ge(sA, A_wabs)
            t.wait_ge(sV, V_onec)
            iT(t.matmul(PSF[0][0:1, 0:8], onecol[:, 0:1], wsum[:, :], start=True, stop=True))
        onT(f1_mm)
        T_bsum = n["T"]

        def f1_chain(v, T_bsum=T_bsum):
            v.wait_ge(sT, T_bsum)
            v.tensor_reduce(fsc[0:1, 0:1], PSF[0][0:1, 0:8], AXX, ALU.add)
            vpad(v)
            v.tensor_scalar(fsc[0:1, 0:1], fsc[0:1, 0:1], 1.0 / (1024.0 * 1024.0), None, ALU.mult)
            vpad(v)
            v.tensor_scalar(fsc[0:1, 1:2], fsc[0:1, 0:1], 1e-5, None, ALU.add)
            vpad(v)
            iV(v.reciprocal(fsc[0:1, 2:3], fsc[0:1, 1:2]))
        onV(f1_chain)
        V_beta = n["V"]

        def f1_rep(t, V_beta=V_beta):
            t.wait_ge(sV, V_beta)
            iT(t.matmul(PSF[2][0:128, 0:1], one_f[0:1, :], fsc[0:1, 2:3], start=True, stop=True))
        onT(f1_rep)
        T_rbrep = n["T"]

        def f1_repc(v, T_rbrep=T_rbrep):
            v.wait_ge(sT, T_rbrep)
            iV(v.tensor_copy(frep[:, 2:3], PSF[2][0:128, 0:1]))
        onV(f1_repc)
        V_rbrep = n["V"]

        ln2_V = {}
        ln2_Aev = {}
        ln2_T = {}
        for tb in range(NB):
            c8 = tb % 8

            def l2_a1(a, tb=tb, c8=c8):
                if tb >= 8:
                    a.wait_ge(sV, ln2_V[tb - 8])
                if tb >= 2:
                    a.wait_ge(sT, ln2_T[tb - 2])
                a.activation(out=scrf[:, 0:1024], in_=x1_sb[:, tb * 1024:tb * 1024 + 1024],
                             func=AF.Identity, accum_out=stat_sb[:, c8:c8 + 1])
                a.activation(out=scrf[:, 0:1024], in_=x1_sb[:, tb * 1024:tb * 1024 + 1024],
                             func=AF.Square, accum_out=stat2_sb[:, c8:c8 + 1])
                iA(a.activation(out=scrf[:, 0:1], in_=stat2_sb[:, c8:c8 + 1], func=AF.Identity))
            onA(l2_a1)
            A_l2 = n["A"]

            def l2_v1(v, c8=c8, A_l2=A_l2):
                v.wait_ge(sA, A_l2)
                v.tensor_scalar(mu_sb[:, c8:c8 + 1], stat_sb[:, c8:c8 + 1],
                                -1.0 / 1024.0, None, ALU.mult)
                v.tensor_scalar(stat3_sb[:, c8:c8 + 1], stat2_sb[:, c8:c8 + 1],
                                1.0 / 1024.0, LN_EPS_FFN, ALU.mult, ALU.add)
                vpad(v)
                v.tensor_tensor(tmp_sb[:, c8:c8 + 1], mu_sb[:, c8:c8 + 1],
                                mu_sb[:, c8:c8 + 1], ALU.mult)
                vpad(v)
                iV(v.tensor_tensor(nms_sb[:, c8:c8 + 1], stat3_sb[:, c8:c8 + 1],
                                   tmp_sb[:, c8:c8 + 1], ALU.subtract))
            onV(l2_v1)
            V_l2 = n["V"]

            def l2_a2(a, c8=c8, V_l2=V_l2):
                a.wait_ge(sV, V_l2)
                iA(a.activation(out=tmp_sb[:, c8:c8 + 1], in_=nms_sb[:, c8:c8 + 1],
                                func=AF.Sqrt))
            onA(l2_a2)
            A_l22 = n["A"]

            def l2_v2(v, c8=c8, A_l22=A_l22):
                v.wait_ge(sA, A_l22)
                v.reciprocal(stat3_sb[:, c8:c8 + 1], tmp_sb[:, c8:c8 + 1])
                vpad(v)
                iV(v.tensor_tensor(nms_sb[:, c8:c8 + 1], mu_sb[:, c8:c8 + 1],
                                   stat3_sb[:, c8:c8 + 1], ALU.mult))
            onV(l2_v2)
            ln2_V[tb] = n["V"]
            V_l22 = n["V"]

            def l2_a3(a, tb=tb, c8=c8, V_l22=V_l22):
                a.wait_ge(sV, V_l22)
                iA(a.activation(out=xnb[:, (tb % 2) * 1024:(tb % 2) * 1024 + 1024],
                                in_=x1_sb[:, tb * 1024:tb * 1024 + 1024], func=AF.Identity,
                                scale=stat3_sb[:, c8:c8 + 1], bias=nms_sb[:, c8:c8 + 1]))
            onA(l2_a3)
            A_l23 = n["A"]

            def l2_tr(t, tb=tb, A_l23=A_l23):
                t.wait_ge(sA, A_l23)
                if tb >= 2:
                    t.wait_ge(sA, ln2_Aev[tb - 2])
                for kc in range(8):
                    mm = t.transpose(PSB[tb % 2][0:128, kc * 128:kc * 128 + 128],
                                     xnb[:, (tb % 2) * 1024 + kc * 128:(tb % 2) * 1024 + kc * 128 + 128],
                                     id_b[:, :])
                    if kc == 7:
                        iT(mm)
            onT(l2_tr)
            ln2_T[tb] = n["T"]
            T_l2 = n["T"]

            def l2_ev(a, tb=tb, T_l2=T_l2):
                a.wait_ge(sT, T_l2)
                if tb == 0:
                    a.wait_ge(sLD, LD_f0)
                for kc in range(8):
                    mm = a.activation(out=sub(xn_sb, kc * 2048 + tb * 128, [[xnrow, 128], [1, 128]]),
                                      in_=PSB[tb % 2][0:128, kc * 128:kc * 128 + 128],
                                      func=AF.Identity, scale=garep[:, kc:kc + 1],
                                      bias=berep[:, kc:kc + 1])
                    if kc == 7:
                        iA(mm)
            onA(l2_ev)
            ln2_Aev[tb] = n["A"]
        A_xn = n["A"]

        wq_V = {}
        for wc in range(8):
            def w2_ld(g, wc=wc):
                if wc == 0:
                    g.wait_ge(sA, wabs_A[7])
                if wc >= 2:
                    g.wait_ge(sV, wq_V[wc - 2])
                iGD(g.dma_start(out=wrot2[:, (wc % 2) * 1024:(wc % 2) * 1024 + 1024],
                                in_=w_ffn[wc * 128:wc * 128 + 128, :]))
            onG(w2_ld)
            GDc = n["GD"]

            def w2_q(a, wc=wc, GDc=GDc):
                a.wait_ge(sGD, GDc)
                if wc == 0:
                    a.wait_ge(sV, V_rbrep)
                a.activation(out=sgnb[:, :], in_=wrot2[:, (wc % 2) * 1024:(wc % 2) * 1024 + 1024],
                             func=AF.Sign)
                iA(a.activation(out=scrf[:, 1024:2048],
                                in_=wrot2[:, (wc % 2) * 1024:(wc % 2) * 1024 + 1024],
                                func=AF.Abs, scale=frep[:, 2:3]))
            onA(w2_q)
            A_w2 = n["A"]

            def w2_v(v, wc=wc, A_w2=A_w2):
                v.wait_ge(sA, A_w2)
                v.tensor_scalar(scrf[:, 1024:2048], scrf[:, 1024:2048], 0.5, None, ALU.is_ge)
                iV(v.tensor_tensor(wq_sb[:, wc * 1024:wc * 1024 + 1024], sgnb[:, :],
                                   scrf[:, 1024:2048], ALU.mult))
            onV(w2_v)
            wq_V[wc] = n["V"]
        V_wq = n["V"]

        def g_red(v):
            v.wait_ge(sA, A_xn)
            for kc in range(8):
                v.tensor_reduce(mxt[:, kc:kc + 1], xn_sb[:, kc * 2048:kc * 2048 + 2048],
                                AXX, ALU.max, apply_absolute_value=True)
            iV(v.engine_nop())
        onV(g_red)
        V_gl = n["V"]

        def g_bounce(s, V_gl=V_gl):
            s.wait_ge(sV, V_gl)
            iST(s.dma_start(out=gmxs_d.ap(), in_=mxt[:, :]))
        onS(g_bounce)
        ST_gb = n["ST"]

        def g_coll(g, ST_gb=ST_gb):
            g.wait_ge(sST, ST_gb)
            iG(g.collective_compute("AllGather", ALU.bypass,
                                    replica_groups=[list(range(8))],
                                    ins=[gmxs_d.ap().opt()], outs=[gmxall_d.ap().opt()]))
        onG(g_coll)
        G_gcoll = n["G"]

        def g_pull2(g, G_gcoll=G_gcoll):
            g.wait_ge(sG, G_gcoll)
            iGD(g.dma_start(out=gmrow[:, :],
                            in_=AP(gmxall_d, 0, [[8192, 1], [1, 8192]])))
        onG(g_pull2)
        GD_gv = n["GD"]

        def g_red2(v, GD_gv=GD_gv):
            v.wait_ge(sGD, GD_gv)
            iV(v.tensor_reduce(fsc[0:1, 4:5], gmrow[0:1, 0:8192], AXX, ALU.max))
        onV(g_red2)
        V_gloc = n["V"]

        def g_chain(v):
            vpad(v)
            v.reciprocal(fsc[0:1, 5:6], fsc[0:1, 4:5])
            v.tensor_tensor(fsc[0:1, 6:7], fsc[0:1, 4:5], fsc[0:1, 0:1], ALU.mult)
            vpad(v)
            v.tensor_scalar(fsc[0:1, 7:8], fsc[0:1, 5:6], 128.0, None, ALU.mult)
            v.tensor_scalar(fsc[0:1, 3:4], fsc[0:1, 6:7], 1.0 / 128.0, None, ALU.mult)
            vpad(v)
            iV(v.engine_nop())
        onV(g_chain)
        V_gch = n["V"]

        def g_rep(t, V_gch=V_gch):
            t.wait_ge(sV, V_gch)
            t.matmul(PSF[3][0:128, 0:1], one_f[0:1, :], fsc[0:1, 7:8], start=True, stop=True)
            iT(t.matmul(PSF[4][0:128, 0:1], one_f[0:1, :], fsc[0:1, 3:4], start=True, stop=True))
        onT(g_rep)
        T_grep = n["T"]

        def g_repc(v, T_grep=T_grep):
            v.wait_ge(sT, T_grep)
            v.tensor_copy(frep[:, 0:1], PSF[3][0:128, 0:1])
            iV(v.tensor_copy(frep[:, 1:2], PSF[4][0:128, 0:1]))
        onV(g_repc)
        V_frep = n["V"]

        if DBG:
            def dbg_ffn(g):
                g.wait_ge(sA, A_xn)
                g.wait_ge(sV, V_wq)
                g.wait_ge(sV, V_frep)
                iGD(g.dma_start(out=dbg["d_xn"], in_=xn_sb[:, :]))
                iGD(g.dma_start(out=dbg["d_wq"], in_=wq_sb[:, :]))
                iGD(g.dma_start(out=dbg["d_fs"], in_=fsc[:, :]))
                iGD(g.dma_start(out=dbg["d_mx"][0:128, 0:8], in_=mxt[:, :]))
                iGD(g.dma_start(out=dbg["d_mx"][0:128, 8:9], in_=onecol[:, :], allow_slow_non_contiguous=True))
            onG(dbg_ffn)
        GD_ffnd = n["GD"]

        xq_A = {}
        xq_V = {}
        for kc in range(8):
            def x5_a(a, kc=kc):
                if kc == 0:
                    a.wait_ge(sV, V_frep)
                    a.wait_ge(sGD, GD_ffnd)
                iA(a.activation(out=xn_sb[:, kc * 2048:kc * 2048 + 2048],
                                in_=xn_sb[:, kc * 2048:kc * 2048 + 2048],
                                func=AF.Identity, scale=frep[:, 0:1]))
            onA(x5_a)
            A_x5 = n["A"]

            def x5_v(v, kc=kc, A_x5=A_x5):
                v.wait_ge(sA, A_x5)
                iV(v.tensor_scalar(xn_sb[:, kc * 2048:kc * 2048 + 2048],
                                   xn_sb[:, kc * 2048:kc * 2048 + 2048],
                                   128.0 - 1e-5, -(128.0 - 1e-5), ALU.min, ALU.max))
            onV(x5_v)
            xq_V[kc] = n["V"]
        V_xq = n["V"]

        ffn_V = {}
        for tb in range(NB):
            def f6_mm(t, tb=tb):
                if tb == 0:
                    t.wait_ge(sV, V_xq)
                    t.wait_ge(sV, V_wq)
                if tb >= 2:
                    t.wait_ge(sV, ffn_V[tb - 2])
                bb = (tb % 2) * 2
                for kc in range(8):
                    for t2 in range(2):
                        mm = t.matmul(PSF[bb + t2][0:128, 0:512],
                                      sub(xn_sb, kc * 2048 + tb * 128, [[xnrow, 128], [1, 128]]),
                                      wq_sb[:, (kc * 2 + t2) * 512:(kc * 2 + t2) * 512 + 512],
                                      start=(kc == 0), stop=(kc == 7))
                        if kc == 7:
                            iT(mm)
            onT(f6_mm)
            T_f6 = n["T"]

            def f6_ev(v, tb=tb, T_f6=T_f6):
                if tb == 0:
                    v.wait_ge(sST, ST_dbg_x1)
                bb = (tb % 2) * 2
                for t2 in range(2):
                    v.wait_ge(sT, T_f6 - 2 + t2 + 1)
                    mm = v.scalar_tensor_tensor(
                        x1_sb[:, tb * 1024 + t2 * 512:tb * 1024 + t2 * 512 + 512],
                        PSF[bb + t2][0:128, 0:512], frep[:, 1:2],
                        x1_sb[:, tb * 1024 + t2 * 512:tb * 1024 + t2 * 512 + 512],
                        ALU.mult, ALU.add)
                    if t2 == 1:
                        iV(mm)
            onV(f6_ev)
            ffn_V[tb] = n["V"]

        ln3_V = {}
        for tb in range(NB):
            c8 = tb % 8

            def l3_a1(a, tb=tb, c8=c8):
                a.wait_ge(sV, ffn_V[tb])
                if tb >= 8:
                    a.wait_ge(sV, ln3_V[tb - 8])
                a.activation(out=scrf[:, 0:1024], in_=x1_sb[:, tb * 1024:tb * 1024 + 1024],
                             func=AF.Identity, accum_out=stat_sb[:, c8:c8 + 1])
                a.activation(out=scrf[:, 0:1024], in_=x1_sb[:, tb * 1024:tb * 1024 + 1024],
                             func=AF.Square, accum_out=stat2_sb[:, c8:c8 + 1])
                iA(a.activation(out=scrf[:, 0:1], in_=stat2_sb[:, c8:c8 + 1], func=AF.Identity))
            onA(l3_a1)
            A_l3 = n["A"]

            def l3_v1(v, c8=c8, A_l3=A_l3):
                v.wait_ge(sA, A_l3)
                v.tensor_scalar(mu_sb[:, c8:c8 + 1], stat_sb[:, c8:c8 + 1],
                                -1.0 / 1024.0, None, ALU.mult)
                v.tensor_scalar(stat3_sb[:, c8:c8 + 1], stat2_sb[:, c8:c8 + 1],
                                1.0 / 1024.0, LN_EPS_1, ALU.mult, ALU.add)
                vpad(v)
                v.tensor_tensor(tmp_sb[:, c8:c8 + 1], mu_sb[:, c8:c8 + 1],
                                mu_sb[:, c8:c8 + 1], ALU.mult)
                vpad(v)
                iV(v.tensor_tensor(nms_sb[:, c8:c8 + 1], stat3_sb[:, c8:c8 + 1],
                                   tmp_sb[:, c8:c8 + 1], ALU.subtract))
            onV(l3_v1)
            V_l3 = n["V"]

            def l3_a2(a, c8=c8, V_l3=V_l3):
                a.wait_ge(sV, V_l3)
                iA(a.activation(out=tmp_sb[:, c8:c8 + 1], in_=nms_sb[:, c8:c8 + 1],
                                func=AF.Sqrt))
            onA(l3_a2)
            A_l32 = n["A"]

            def l3_v2(v, c8=c8, A_l32=A_l32):
                v.wait_ge(sA, A_l32)
                v.reciprocal(stat3_sb[:, c8:c8 + 1], tmp_sb[:, c8:c8 + 1])
                vpad(v)
                iV(v.tensor_tensor(nms_sb[:, c8:c8 + 1], mu_sb[:, c8:c8 + 1],
                                   stat3_sb[:, c8:c8 + 1], ALU.mult))
            onV(l3_v2)
            ln3_V[tb] = n["V"]
            V_l32 = n["V"]

            def l3_a3(a, tb=tb, c8=c8, V_l32=V_l32):
                a.wait_ge(sV, V_l32)
                iA(a.activation(out=x1_sb[:, tb * 1024:tb * 1024 + 1024],
                                in_=x1_sb[:, tb * 1024:tb * 1024 + 1024], func=AF.Identity,
                                scale=stat3_sb[:, c8:c8 + 1], bias=nms_sb[:, c8:c8 + 1]))
            onA(l3_a3)
        A_fin = n["A"]

        def out_st(s):
            s.wait_ge(sA, A_fin)
            for tb in range(NB):
                iST(s.dma_start(out=y_out[tb * 128:tb * 128 + 128, :],
                                in_=x1_sb[:, tb * 1024:tb * 1024 + 1024]))
            s.wait_ge(sST, n["ST"])
        onS(out_st)
        ctxF2.close()
        ctx3.close()

    return nc, dbg


_CACHE = {}


def kernel(L0_attn_x, L0_attn_w_qkv, L0_attn_b_qkv, L0_attn_w_out, L0_attn_b_out,
           L0_attn_w_gate, L0_attn_b_gate, L0_attn_pos_bias, L0_ffn_weight,
           L0_ffn_ln_gamma, L0_ffn_ln_beta):
    if "nc" not in _CACHE:
        _CACHE["nc"], _CACHE["dbg"] = build()
    nc = _CACHE["nc"]
    c = _hostc()
    x = np.ascontiguousarray(np.asarray(L0_attn_x, dtype=np.float32))
    wqkv = np.ascontiguousarray(np.asarray(L0_attn_w_qkv, dtype=np.float32))
    bqkv = np.asarray(L0_attn_b_qkv, dtype=np.float32)
    bq_fm = np.ascontiguousarray(bqkv.reshape(24, 128).T)
    bv_r = np.ascontiguousarray(bqkv[2048:3072].reshape(1, 1024))
    bg_fm = np.ascontiguousarray(np.asarray(L0_attn_b_gate, dtype=np.float32).reshape(8, 128).T)
    ims = []
    for core in range(8):
        b = core // 4
        p4 = core % 4
        im = {
            "x_in": np.ascontiguousarray(x[b, p4 * 2048:(p4 + 1) * 2048, :]),
            "w_qkv": wqkv,
            "b_qkv": bq_fm,
            "bv_row": bv_r,
            "w_out_d": np.ascontiguousarray(np.asarray(L0_attn_w_out, dtype=np.float32)),
            "b_out": np.ascontiguousarray(np.asarray(L0_attn_b_out, dtype=np.float32).reshape(1, 1024)),
            "w_gate": np.ascontiguousarray(np.asarray(L0_attn_w_gate, dtype=np.float32)),
            "b_gate": bg_fm,
            "pos_b": np.ascontiguousarray(np.asarray(L0_attn_pos_bias, dtype=np.float32)),
            "w_ffn": np.ascontiguousarray(np.asarray(L0_ffn_weight, dtype=np.float32)),
            "g_ffn": np.ascontiguousarray(np.asarray(L0_ffn_ln_gamma, dtype=np.float32).reshape(8, 128).T),
            "be_ffn": np.ascontiguousarray(np.asarray(L0_ffn_ln_beta, dtype=np.float32).reshape(8, 128).T),
            "ident_b": c["ident_b"],
            "ones_b": c["ones_b"],
            "ones_f": c["ones_f"],
            "ind_n": c["ind_n"],
            "ind_f": c["ind_f"],
            "sel2_in": c["sel2_in"],
            "rowsel_in": c["rowsel_in"],
        }
        ims.append(im)
    res = run_bass_kernel_spmd(nc, ims, core_ids=list(range(8)))
    out = np.zeros((B_, N_, D_), dtype=np.float32)
    for core in range(8):
        b = core // 4
        p4 = core % 4
        out[b, p4 * 2048:(p4 + 1) * 2048, :] = np.asarray(res.results[core]["y_out"])
    _CACHE["res"] = res
    return out
